# version 11
# baseline (speedup 1.0000x reference)
import numpy as np
from contextlib import ExitStack

import concourse.bass as bass
import concourse.mybir as mybir
from concourse.bass_utils import run_bass_kernel_spmd
from concourse.alu_op_type import AluOpType as OP

F32 = mybir.dt.float32
BF16 = mybir.dt.bfloat16
I32 = mybir.dt.int32
AF = mybir.ActivationFunctionType
AX = mybir.AxisListType

B, S, D = 4, 4096, 1024
NH, HS = 16, 64
H = 4096
QO = 2048
NKT = S // 128
NTT = QO // 128
EPS_LN = 1e-5
EPS_Q = 1e-5
VSTRIDE = HS + 1
VTILE = NH * VSTRIDE

NEGM = 0
BQ = 16
BK = 24
S1C = 32
WS = 48
ST = 56
MQ2 = 64
C_T1, C_NT1, C_T2, C_NT2 = WS + 2, WS + 3, WS + 4, WS + 5
C_WQ1, C_WQ2 = WS + 6, WS + 7


class Ctr:
    def __init__(self, sem, step=1):
        self.sem, self.step, self.n = sem, step, 0

    def inc(self, inst):
        inst.then_inc(self.sem, self.step)
        self.n += self.step
        return self.n


def build_nc(debug=False):
    nc = bass.Bass("TRN2", target_bir_lowering=False, debug=False,
                   detect_race_conditions=False)

    xT = nc.dram_tensor("xT", [D, S], F32, kind="ExternalInput").ap()
    xTq = nc.dram_tensor("xTq", [D, QO], F32, kind="ExternalInput").ap()
    x_own = nc.dram_tensor("x_own", [QO, D], F32, kind="ExternalInput").ap()
    wqkv = nc.dram_tensor("wqkv", [D, 3 * D], F32, kind="ExternalInput").ap()
    bqkv = nc.dram_tensor("bqkv", [3 * D], F32, kind="ExternalInput").ap()
    wo = nc.dram_tensor("wo", [D, D], F32, kind="ExternalInput").ap()
    bo = nc.dram_tensor("bo", [D], F32, kind="ExternalInput").ap()
    w1t = nc.dram_tensor("w1t", [D, H], F32, kind="ExternalInput").ap()
    b1 = nc.dram_tensor("b1", [H], F32, kind="ExternalInput").ap()
    w2t = nc.dram_tensor("w2t", [H, D], F32, kind="ExternalInput").ap()
    b2 = nc.dram_tensor("b2", [D], F32, kind="ExternalInput").ap()
    ident_in = nc.dram_tensor("ident_in", [128, 128], BF16, kind="ExternalInput").ap()
    y = nc.dram_tensor("y", [QO, D], F32, kind="ExternalOutput").ap()
    if debug:
        dbg_wsm = nc.dram_tensor("dbg_wsm", [128, 192], F32, kind="ExternalOutput").ap()
        dbg_x1 = nc.dram_tensor("dbg_x1", [QO, D], F32, kind="ExternalOutput").ap()
        dbg_wf = nc.dram_tensor("dbg_wf", [128, 5632], F32, kind="ExternalOutput").ap()
        dbg_wo = nc.dram_tensor("dbg_wo", [8, 128, D], BF16, kind="ExternalOutput").ap()
        dbg_q = nc.dram_tensor("dbg_q", [128, 8 * QO], BF16, kind="ExternalOutput").ap()
        dbg_k = nc.dram_tensor("dbg_k", [128, 8 * 4096], BF16, kind="ExternalOutput").ap()
        dbg_v = nc.dram_tensor("dbg_v", [128, NKT * VTILE], BF16, kind="ExternalOutput").ap()
        dbg_av = nc.dram_tensor("dbg_av", [128, 8 * QO], BF16, kind="ExternalOutput").ap()

    dscr = nc.dram_tensor("dscr", [1, 1024], F32, kind="Internal").ap()
    mscr = nc.dram_tensor("mscr", [16], F32, kind="Internal").ap()
    sscr = nc.dram_tensor("sscr", [8], F32, kind="Internal").ap()
    rscr = nc.dram_tensor("rscr", [128], F32, kind="Internal").ap()
    wo_bf = nc.dram_tensor("wo_bf", [8, 128, D], BF16, kind="Internal").ap()
    x1scr = nc.dram_tensor("x1scr", [QO, D], F32, kind="Internal").ap()

    with ExitStack() as ctx:
        e = ctx.enter_context
        karena = e(nc.sbuf_tensor("karena", [128, 8 * 4096], BF16))
        varena = e(nc.sbuf_tensor("varena", [128, NKT * VTILE], BF16))
        qarena = e(nc.sbuf_tensor("qarena", [128, 8 * QO], BF16))
        wkA = e(nc.sbuf_tensor("wkA", [128, 4096], BF16))
        wkB = e(nc.sbuf_tensor("wkB", [128, 5632], BF16))
        wf = e(nc.sbuf_tensor("wf", [128, 5632], F32))
        wsm = e(nc.sbuf_tensor("wsm", [128, 192], F32))
        wri = e(nc.sbuf_tensor("wri", [128, 512], I32))
        ident = e(nc.sbuf_tensor("ident", [128, 128], BF16))
        wsb = e(nc.sbuf_tensor("wsb", [128, 32], BF16))
        b2row = e(nc.sbuf_tensor("b2row", [1, 1024], BF16))
        s2invrow = e(nc.sbuf_tensor("s2invrow", [1, 128], BF16))

        psBIG = e(nc.psum_tensor("psBIG", [128, 2048], F32))
        psC = e(nc.psum_tensor("psC", [128, 1024], F32))
        psTa = e(nc.psum_tensor("psTa", [128, 512], BF16))
        psTb = e(nc.psum_tensor("psTb", [128, 512], BF16))

        qLD = Ctr(e(nc.semaphore("qLD")), 16)
        qLG = Ctr(e(nc.semaphore("qLG")), 16)
        qOUT = Ctr(e(nc.semaphore("qOUT")), 16)
        cPE = Ctr(e(nc.semaphore("cPE")))
        cDV = Ctr(e(nc.semaphore("cDV")))
        cAC = Ctr(e(nc.semaphore("cAC")))

        block = e(nc.Block())

        @block.sync
        def _(sync):
            qLD.inc(sync.dma_start(out=ident[:], in_=ident_in[:]))
            for fc in range(8):
                qLD.inc(sync.dma_start(
                    out=wsm[:, BQ + fc:BQ + fc + 1],
                    in_=bqkv[fc * 128:(fc + 1) * 128].rearrange("(p a) -> p a", a=1)))
                qLD.inc(sync.dma_start(
                    out=wsm[:, BK + fc:BK + fc + 1],
                    in_=bqkv[1024 + fc * 128:1024 + (fc + 1) * 128].rearrange("(p a) -> p a", a=1)))
            inst = sync.dma_start(out=wf[:, 2048:3072],
                                  in_=bqkv[2048:3072].rearrange("(a n) -> a n", a=1)
                                  .partition_broadcast(128))
            qLD.inc(inst)
        LD_CONST = qLD.n

        @block.vector
        def _(vector):
            for inst in [vector.memset(wsm[:, NEGM:NEGM + 1], -16.0),
                         vector.memset(wsb[:, :], 0.0),
                         vector.memset(wsb[0:64, 0:1], 1.0),
                         vector.memset(wsb[64:128, 1:2], 1.0),
                         vector.memset(wsb[:, 2:3], 1.0),
                         vector.memset(wsm[:, 96:97], 1.0),
                         vector.memset(varena[:, :], 1.0)]:
                cDV.inc(inst)
                vector.wait_ge(cDV.sem, cDV.n)
        DV_CONST = cDV.n

        w_pieces = [(w1t, c, p, 0) for c in range(8) for p in range(4)] + \
                   [(w2t, c, 0, 1) for c in range(32)]
        NW = len(w_pieces)
        lg0 = qLG.n
        dv0 = cDV.n
        pe0 = cPE.n

        @block.gpsimd
        def _(g):
            for i in range(NW):
                src, c, p, _w = w_pieces[i]
                if i >= 1:
                    g.wait_ge(cDV.sem, dv0 + i * 2)
                inst = g.dma_start(
                    out=wf[:, 3072 + (i % 2) * 1024:3072 + (i % 2) * 1024 + 1024],
                    in_=src[c * 128:(c + 1) * 128, p * 1024:(p + 1) * 1024])
                qLG.inc(inst)

        @block.vector
        def _(vector):
            for i in range(NW):
                src, c, p, w = w_pieces[i]
                vector.wait_ge(qLG.sem, lg0 + (i + 1) * 16)
                inst = vector.reduce_sum(
                    wsm[:, ST + 4:ST + 5],
                    wf[:, 3072 + (i % 2) * 1024:3072 + (i % 2) * 1024 + 1024],
                    AX.X, apply_absolute_value=True)
                cDV.inc(inst)
                vector.wait_ge(cDV.sem, cDV.n)
                acc = wsm[:, ST + 5:ST + 6] if w == 0 else wsm[:, ST + 6:ST + 7]
                if i == 0 or i == 32:
                    inst = vector.tensor_copy(acc, wsm[:, ST + 4:ST + 5])
                else:
                    inst = vector.tensor_add(acc, acc, wsm[:, ST + 4:ST + 5])
                cDV.inc(inst)
                vector.wait_ge(cDV.sem, cDV.n)
        assert cDV.n == dv0 + NW * 2
        DV_W0 = cDV.n

        @block.tensor
        def _(t):
            t.wait_ge(cDV.sem, DV_W0)
            t.matmul(psC[0:1, 0:1], wsm[:, 96:97], wsm[:, ST + 5:ST + 6],
                     start=True, stop=True)
            inst = t.matmul(psC[0:1, 512:513], wsm[:, 96:97], wsm[:, ST + 6:ST + 7],
                            start=True, stop=True)
            cPE.inc(inst)
        PE_W0 = cPE.n
        assert PE_W0 == pe0 + 1

        @block.vector
        def _(vector):
            vector.wait_ge(cPE.sem, PE_W0)
            for inst in [
                vector.tensor_copy(wsm[0:1, ST:ST + 1], psC[0:1, 0:1]),
                vector.tensor_copy(wsm[0:1, ST + 1:ST + 2], psC[0:1, 512:513]),
                vector.tensor_scalar(wsm[0:1, ST:ST + 1], wsm[0:1, ST:ST + 1],
                                     1.0 / (D * H), EPS_Q, OP.mult, OP.max),
                vector.tensor_scalar(wsm[0:1, ST + 1:ST + 2], wsm[0:1, ST + 1:ST + 2],
                                     1.0 / (D * H), EPS_Q, OP.mult, OP.max),
                vector.tensor_scalar_mul(wsm[0:1, ST + 2:ST + 3], wsm[0:1, ST:ST + 1], 0.5),
                vector.tensor_scalar_mul(wsm[0:1, ST + 3:ST + 4], wsm[0:1, ST:ST + 1], -0.5),
                vector.tensor_scalar_mul(wsm[0:1, ST + 4:ST + 5], wsm[0:1, ST + 1:ST + 2], 0.5),
                vector.tensor_scalar_mul(wsm[0:1, ST + 5:ST + 6], wsm[0:1, ST + 1:ST + 2], -0.5),
                vector.tensor_scalar_mul(wsm[0:1, ST + 6:ST + 7], wsm[0:1, ST:ST + 1], 1.0 / 127.0),
                vector.tensor_scalar_mul(wsm[0:1, ST + 7:ST + 8], wsm[0:1, ST + 1:ST + 2],
                                         1.0 / 127.0),
            ]:
                cDV.inc(inst)
                vector.wait_ge(cDV.sem, cDV.n)
        DV_WS = cDV.n

        @block.gpsimd
        def _(g):
            g.wait_ge(cDV.sem, DV_WS)
            inst = g.dma_start(out=sscr.rearrange("(a n) -> a n", a=1), in_=wsm[0:1, ST:ST + 8])
            qLG.inc(inst)
            g.wait_ge(qLG.sem, qLG.n)
            inst = g.dma_start(out=wsm[:, WS:WS + 8],
                               in_=sscr.rearrange("(a n) -> a n", a=1).partition_broadcast(128))
            qLG.inc(inst)
        LG_WS = qLG.n

        def emit_proj(sub, xsrc, ntb, wcol0):
            ld0 = qLD.n
            dv0 = cDV.n
            pe0 = cPE.n

            def x_dma(tb, dc):
                return ld0 + (tb * 24 + dc + 1) * 16

            def w_dma(tb, fh, dc):
                return ld0 + (tb * 24 + 8 + fh * 8 + dc + 1) * 16

            def x_cast(tb, dc):
                return dv0 + tb * 32 + dc + 1

            def w_cast(tb, fh, dc):
                return dv0 + tb * 32 + 8 + fh * 12 + dc + 1

            def evac(tb, fh, g):
                return dv0 + tb * 32 + 8 + fh * 12 + 8 + g + 1

            def pe_grp(tb, fh, g):
                return pe0 + (tb * 2 + fh) * 4 + g + 1

            @block.sync
            def _(sync):
                for tb in range(ntb):
                    for dc in range(8):
                        i = tb * 8 + dc
                        if i >= 1:
                            pb_, pdc = divmod(i - 1, 8)
                            sync.wait_ge(cDV.sem, x_cast(pb_, pdc))
                        inst = sync.dma_start(
                            out=wf[:, (i % 2) * 512:(i % 2) * 512 + 512],
                            in_=xsrc[dc * 128:(dc + 1) * 128, tb * 512:tb * 512 + 512])
                        qLD.inc(inst)
                    for fh in range(2):
                        for dc in range(8):
                            j = (tb * 2 + fh) * 8 + dc
                            if j >= 1:
                                pj = j - 1
                                ptb, pr = divmod(pj, 16)
                                pfh, pdc = divmod(pr, 8)
                                sync.wait_ge(cDV.sem, w_cast(ptb, pfh, pdc))
                            inst = sync.dma_start(
                                out=wf[:, 1024 + (j % 2) * 512:1024 + (j % 2) * 512 + 512],
                                in_=wqkv[dc * 128:(dc + 1) * 128,
                                         wcol0 + fh * 512:wcol0 + fh * 512 + 512])
                            qLD.inc(inst)

            @block.vector
            def _(vector):
                for tb in range(ntb):
                    for dc in range(8):
                        vector.wait_ge(qLD.sem, x_dma(tb, dc))
                        if tb == 0 and dc == 0:
                            vector.wait_ge(cPE.sem, pe0)
                        if tb >= 1 and dc == 0:
                            vector.wait_ge(cPE.sem, pe_grp(tb - 1, 1, 3))
                        inst = vector.tensor_copy(
                            wkB[:, dc * 512:dc * 512 + 512],
                            wf[:, ((tb * 8 + dc) % 2) * 512:((tb * 8 + dc) % 2) * 512 + 512])
                        cDV.inc(inst)
                        assert cDV.n == x_cast(tb, dc)
                    for fh in range(2):
                        for dc in range(8):
                            vector.wait_ge(qLD.sem, w_dma(tb, fh, dc))
                            if dc == 0:
                                if tb * 2 + fh >= 1:
                                    pfh = tb * 2 + fh - 1
                                    vector.wait_ge(cPE.sem, pe_grp(pfh // 2, pfh % 2, 3))
                                else:
                                    vector.wait_ge(cPE.sem, pe0)
                            j = (tb * 2 + fh) * 8 + dc
                            inst = vector.tensor_copy(
                                wkA[:, dc * 512:dc * 512 + 512],
                                wf[:, 1024 + (j % 2) * 512:1024 + (j % 2) * 512 + 512])
                            cDV.inc(inst)
                            assert cDV.n == w_cast(tb, fh, dc)
                        for g in range(4):
                            vector.wait_ge(cPE.sem, pe_grp(tb, fh, g))
                            q = (tb * 2 + fh) * 4 + g
                            ps = psBIG[:, (q % 4) * 512:(q % 4) * 512 + 512]
                            if sub == "K":
                                fc = fh * 4 + g
                                inst = vector.tensor_scalar(
                                    karena[:, fc * 4096 + tb * 512:fc * 4096 + tb * 512 + 512],
                                    ps, wsm[:, BK + fc:BK + fc + 1], None, OP.add)
                            elif sub == "Q":
                                fc = fh * 4 + g
                                inst = vector.tensor_scalar(
                                    qarena[:, fc * 2048 + tb * 512:fc * 2048 + tb * 512 + 512],
                                    ps, wsm[:, BQ + fc:BQ + fc + 1], None, OP.add)
                            else:
                                kt = tb * 4 + g
                                h0 = fh * 8
                                dst = varena[:, kt * VTILE:(kt + 1) * VTILE] \
                                    .rearrange("p (h v) -> p h v", v=VSTRIDE)[:, h0:h0 + 8, 0:64]
                                inst = vector.tensor_tensor(
                                    dst,
                                    ps.rearrange("p (h v) -> p h v", v=64),
                                    wf[:, 2048 + fh * 512:2048 + fh * 512 + 512]
                                    .rearrange("p (h v) -> p h v", v=64),
                                    OP.add)
                            cDV.inc(inst)
                            assert cDV.n == evac(tb, fh, g)

            @block.tensor
            def _(t):
                for tb in range(ntb):
                    for fh in range(2):
                        for g in range(4):
                            q = (tb * 2 + fh) * 4 + g
                            ps = psBIG[:, (q % 4) * 512:(q % 4) * 512 + 512]
                            if q >= 4:
                                pq = q - 4
                                t.wait_ge(cDV.sem, evac(pq // 8, (pq // 4) % 2, pq % 4))
                            for dc in range(8):
                                t.wait_ge(cDV.sem, max(x_cast(tb, dc), w_cast(tb, fh, dc)))
                                if sub in ("K", "Q"):
                                    inst = t.matmul(
                                        ps,
                                        wkA[:, dc * 512 + g * 128:dc * 512 + g * 128 + 128],
                                        wkB[:, dc * 512:dc * 512 + 512],
                                        start=(dc == 0), stop=(dc == 7))
                                else:
                                    inst = t.matmul(
                                        ps,
                                        wkB[:, dc * 512 + g * 128:dc * 512 + g * 128 + 128],
                                        wkA[:, dc * 512:dc * 512 + 512],
                                        start=(dc == 0), stop=(dc == 7))
                                if dc == 7:
                                    cPE.inc(inst)
                            assert cPE.n == pe_grp(tb, fh, g)

            qLD.n = ld0 + ntb * 24 * 16
            assert cDV.n == dv0 + ntb * 32
            assert cPE.n == pe0 + ntb * 8

        emit_proj("K", xT, 8, 1024)
        emit_proj("V", xT, 8, 2048)
        emit_proj("Q", xTq, 4, 0)
        PE_PROJ = cPE.n
        DV_PROJ = cDV.n

        pe0 = cPE.n
        ac0 = cAC.n
        dv0 = cDV.n
        lg0 = qLG.n
        steps = [(h, qc) for h in range(NH) for qc in range(2)]
        NSTEP = len(steps)

        def st_inc(gkt):
            return pe0 + gkt * 2 + 1

        def pv_inc(gkt):
            return pe0 + gkt * 2 + 2

        def exp_inc(gkt):
            return ac0 + gkt + 1

        def r_inc(m):
            return dv0 + m * 2 + 1

        def tt_inc(m):
            return dv0 + m * 2 + 2

        def lg_back(m):
            return lg0 + (m * 3 + 2) * 16

        def lg_move(m):
            return lg0 + (m * 3 + 3) * 16

        @block.tensor
        def _(t):
            t.wait_ge(cDV.sem, DV_WS)
            t.wait_ge(cDV.sem, DV_PROJ)
            for m, (h, qc) in enumerate(steps):
                ch, rh = h // 2, (h % 2) * 64
                for kt in range(NKT):
                    gkt = m * NKT + kt
                    stb = (gkt % 2) * 1024
                    if gkt >= 2:
                        t.wait_ge(cAC.sem, exp_inc(gkt - 2))
                    lhsK = karena[rh:rh + 64, ch * 4096 + kt * 128:ch * 4096 + kt * 128 + 128]
                    for half in range(2):
                        inst = t.matmul(psBIG[:, stb + half * 512:stb + half * 512 + 512],
                                        lhsK,
                                        qarena[rh:rh + 64,
                                               ch * 2048 + qc * 1024 + half * 512:
                                               ch * 2048 + qc * 1024 + half * 512 + 512],
                                        start=True, stop=True)
                    cPE.inc(inst)
                    assert cPE.n == st_inc(gkt)
                    t.wait_ge(cAC.sem, exp_inc(gkt))
                    if kt == 0 and m >= 1:
                        t.wait_ge(cDV.sem, tt_inc(m - 1))
                    lhsV = varena[:, kt * VTILE + h * VSTRIDE:kt * VTILE + h * VSTRIDE + VSTRIDE]
                    pt = wkB[:, (gkt % 2) * 1024:(gkt % 2) * 1024 + 1024]
                    for half in range(2):
                        inst = t.matmul(psC[0:65, half * 512:half * 512 + 512],
                                        lhsV, pt[:, half * 512:half * 512 + 512],
                                        start=(kt == 0), stop=(kt == NKT - 1))
                    cPE.inc(inst)
                    assert cPE.n == pv_inc(gkt)

        @block.scalar
        def _(scalar):
            for m, (h, qc) in enumerate(steps):
                for kt in range(NKT):
                    gkt = m * NKT + kt
                    stb = (gkt % 2) * 1024
                    scalar.wait_ge(cPE.sem, st_inc(gkt))
                    if gkt >= 2:
                        scalar.wait_ge(cPE.sem, pv_inc(gkt - 2))
                    inst = scalar.activation(wkB[:, (gkt % 2) * 1024:(gkt % 2) * 1024 + 1024],
                                             psBIG[:, stb:stb + 1024], AF.Exp,
                                             bias=wsm[:, NEGM:NEGM + 1], scale=0.125)
                    cAC.inc(inst)
                    assert cAC.n == exp_inc(gkt)

        @block.vector
        def _(vector):
            for m, (h, qc) in enumerate(steps):
                vector.wait_ge(cPE.sem, pv_inc(m * NKT + NKT - 1))
                inst = vector.reciprocal(wf[64:65, 3072:4096], psC[64:65, 0:1024])
                cDV.inc(inst)
                assert cDV.n == r_inc(m)
                vector.wait_ge(qLG.sem, lg_back(m))
                if m >= 1:
                    vector.wait_ge(qLG.sem, lg_move(m - 1))
                inst = vector.tensor_tensor(wkB[0:64, 2048:3072], psC[0:64, 0:1024],
                                            wf[0:64, 3072:4096], OP.mult)
                cDV.inc(inst)
                assert cDV.n == tt_inc(m)

        @block.gpsimd
        def _(g):
            for m, (h, qc) in enumerate(steps):
                ch, rh = h // 2, (h % 2) * 64
                g.wait_ge(cDV.sem, r_inc(m))
                inst = g.dma_start(out=dscr[:], in_=wf[64:65, 3072:4096])
                qLG.inc(inst)
                g.wait_ge(qLG.sem, qLG.n)
                if m >= 1:
                    g.wait_ge(cDV.sem, tt_inc(m - 1))
                inst = g.dma_start(out=wf[0:64, 3072:4096], in_=dscr.partition_broadcast(64))
                qLG.inc(inst)
                g.wait_ge(cDV.sem, tt_inc(m))
                inst = g.dma_start(
                    out=qarena[rh:rh + 64, ch * 2048 + qc * 1024:ch * 2048 + qc * 1024 + 1024],
                    in_=wkB[0:64, 2048:3072])
                qLG.inc(inst)

        assert cPE.n == pe0 + NSTEP * NKT * 2
        assert cAC.n == ac0 + NSTEP * NKT
        assert cDV.n == dv0 + NSTEP * 2
        assert qLG.n == lg0 + NSTEP * 3 * 16
        PE_ATT = cPE.n
        LG_ATT = qLG.n

        if debug:
            @block.sync
            def _(sync):
                sync.wait_ge(cPE.sem, PE_ATT)
                sync.wait_ge(qLG.sem, LG_ATT)
                qLD.inc(sync.dma_start(out=dbg_k[:], in_=karena[:, :]))
                qLD.inc(sync.dma_start(out=dbg_v[:], in_=varena[:, :]))
                qLD.inc(sync.dma_start(out=dbg_av[:], in_=qarena[:, :]))
        DBG_DUMP = qLD.n
        lg0 = qLG.n
        dv0 = cDV.n
        tern_pieces = [(w1t, c, p, 0) for c in range(8) for p in range(4)] + \
                      [(w2t, c, 0, 1) for c in range(32)]
        NT = len(tern_pieces)

        @block.gpsimd
        def _(g):
            g.wait_ge(cPE.sem, PE_ATT)
            if debug:
                g.wait_ge(qLD.sem, DBG_DUMP)
            for i in range(NT):
                src, c, p, w = tern_pieces[i]
                if i >= 1:
                    g.wait_ge(cDV.sem, dv0 + i * 2)
                inst = g.dma_start(
                    out=wf[:, (i % 2) * 1024:(i % 2) * 1024 + 1024],
                    in_=src[c * 128:(c + 1) * 128, p * 1024:(p + 1) * 1024])
                qLG.inc(inst)

        @block.vector
        def _(vector):
            for i in range(NT):
                src, c, p, w = tern_pieces[i]
                vector.wait_ge(qLG.sem, lg0 + (i + 1) * 16)
                stage = wf[:, (i % 2) * 1024:(i % 2) * 1024 + 1024]
                thr = wsm[:, (C_T1 if w == 0 else C_T2):(C_T1 if w == 0 else C_T2) + 1]
                nthr = wsm[:, (C_NT1 if w == 0 else C_NT2):(C_NT1 if w == 0 else C_NT2) + 1]
                inst = vector.tensor_scalar(wf[:, 2048:3072], stage, nthr, None, OP.is_lt)
                cDV.inc(inst)
                if w == 0:
                    dst = karena[:, c * 4096 + p * 1024:c * 4096 + p * 1024 + 1024]
                else:
                    dst = varena[:, c * 1024:(c + 1) * 1024]
                inst = vector.scalar_tensor_tensor(dst, stage, thr, wf[:, 2048:3072],
                                                   OP.is_gt, OP.subtract)
                cDV.inc(inst)
        assert cDV.n == dv0 + NT * 2
        qLG.n = lg0 + NT * 16
        DV_TERN = cDV.n

        lg0 = qLG.n
        dv0 = cDV.n

        @block.gpsimd
        def _(g):
            for c in range(8):
                if c >= 1:
                    g.wait_ge(cDV.sem, dv0 + c)
                inst = g.dma_start(out=wf[:, 0:1024], in_=wo[c * 128:(c + 1) * 128, :])
                qLG.inc(inst)
                g.wait_ge(cDV.sem, dv0 + c + 1)
                inst = g.dma_start(out=wo_bf[c], in_=wkB[:, 0:1024])
                qLG.inc(inst)
            for j in range(8):
                g.wait_ge(cDV.sem, dv0 + 8 + j)
                inst = g.dma_start(out=wf[:, 0:512],
                                   in_=b1[j * 512:(j + 1) * 512]
                                   .rearrange("(a n) -> a n", a=1).partition_broadcast(128))
                qLG.inc(inst)
            g.wait_ge(cDV.sem, dv0 + 16)
            inst = g.dma_start(out=wf[0:1, 0:1024],
                               in_=b2.rearrange("(a n) -> a n", a=1))
            qLG.inc(inst)
            g.wait_ge(cDV.sem, dv0 + 17)
            inst = g.dma_start(out=wf[:, 1024:2048],
                               in_=bo.rearrange("(a n) -> a n", a=1).partition_broadcast(128))
            qLG.inc(inst)

        @block.vector
        def _(vector):
            for c in range(8):
                vector.wait_ge(qLG.sem, lg0 + (2 * c + 1) * 16)
                inst = vector.tensor_copy(wkB[:, 0:1024], wf[:, 0:1024])
                cDV.inc(inst)
            for j in range(8):
                vector.wait_ge(qLG.sem, lg0 + (16 + j + 1) * 16)
                inst = vector.tensor_copy(wkA[:, j * 512:(j + 1) * 512], wf[:, 0:512])
                cDV.inc(inst)
            vector.wait_ge(qLG.sem, lg0 + 25 * 16)
            inst = vector.tensor_copy(b2row[0:1, :], wf[0:1, 0:1024])
            cDV.inc(inst)
        assert cDV.n == dv0 + 17
        qLG.n = lg0 + 26 * 16
        DV_W = cDV.n
        LG_W = qLG.n

        ld0 = qLD.n
        lg0 = qLG.n
        pe0 = cPE.n
        dv0 = cDV.n
        ac0 = cAC.n

        def pA_xdma(t):
            return ld0 + (t * 3 + 1) * 16

        def pA_wo(t, dh):
            return ld0 + (t * 3 + 2 + dh) * 16

        def pA_pe(t, k):
            return pe0 + t * 3 + k

        def pA_dv(t, k):
            return dv0 + t * 32 + k

        def pA_ac(t, k):
            return ac0 + t * 2 + k

        def pA_x1dma(t):
            return lg0 + (t + 1) * 16

        @block.sync
        def _(sync):
            sync.wait_ge(qLG.sem, LG_W)
            for t in range(NTT):
                if t >= 1:
                    sync.wait_ge(cDV.sem, pA_dv(t - 1, 1))
                inst = sync.dma_start(out=wf[:, 0:1024],
                                      in_=x_own[t * 128:(t + 1) * 128, :])
                qLD.inc(inst)
                for dh in range(2):
                    if t * 2 + dh >= 1:
                        pt_, pdh = divmod(t * 2 + dh - 1, 2)
                        sync.wait_ge(cPE.sem, pA_pe(pt_, pdh + 1))
                    inst = sync.dma_start(
                        out=wkB[:, 0:4096].rearrange("p (c x) -> p c x", x=512),
                        in_=wo_bf[:, :, dh * 512:dh * 512 + 512].rearrange("c p x -> p c x"))
                    qLD.inc(inst)

        @block.tensor
        def _(t_):
            t_.wait_ge(qLG.sem, LG_ATT)
            if debug:
                t_.wait_ge(qLD.sem, DBG_DUMP)
            for t in range(NTT):
                ps = psBIG[:, (t % 2) * 1024:(t % 2) * 1024 + 1024]
                if t >= 2:
                    t_.wait_ge(cDV.sem, pA_dv(t - 2, 2))
                for dh in range(2):
                    t_.wait_ge(qLD.sem, pA_wo(t, dh))
                    for c in range(8):
                        inst = t_.matmul(ps[:, dh * 512:dh * 512 + 512],
                                         qarena[:, c * 2048 + t * 128:c * 2048 + t * 128 + 128],
                                         wkB[:, c * 512:c * 512 + 512],
                                         start=(c == 0), stop=(c == 7))
                    cPE.inc(inst)
                    assert cPE.n == pA_pe(t, dh + 1)
                t_.wait_ge(cDV.sem, pA_dv(t, 30))
                if t >= 1:
                    t_.wait_ge(cDV.sem, pA_dv(t - 1, 32))
                for dc in range(8):
                    dst = psTa if dc < 4 else psTb
                    inst = t_.transpose(dst[:, (dc % 4) * 128:(dc % 4) * 128 + 128],
                                        wkB[:, 4096 + dc * 128:4096 + dc * 128 + 128],
                                        ident[:])
                cPE.inc(inst)
                assert cPE.n == pA_pe(t, 3)

        @block.vector
        def _(vector):
            for t in range(NTT):
                k = 0
                vector.wait_ge(qLD.sem, pA_xdma(t))
                inst = vector.tensor_add(wf[:, 2048:3072], wf[:, 0:1024], wf[:, 1024:2048])
                k += 1; cDV.inc(inst)
                vector.wait_ge(cPE.sem, pA_pe(t, 2))
                ps = psBIG[:, (t % 2) * 1024:(t % 2) * 1024 + 1024]
                inst = vector.tensor_add(wf[:, 2048:3072], ps, wf[:, 2048:3072])
                k += 1; cDV.inc(inst)
                inst = vector.reduce_sum(wsm[:, ST:ST + 1], wf[:, 2048:3072], AX.X)
                k += 1; cDV.inc(inst); vector.wait_ge(cDV.sem, cDV.n)
                inst = vector.tensor_scalar_mul(wsm[:, ST + 1:ST + 2], wsm[:, ST:ST + 1], -1.0 / D)
                k += 1; cDV.inc(inst); vector.wait_ge(cDV.sem, cDV.n)
                if t >= 1:
                    vector.wait_ge(qLG.sem, pA_x1dma(t - 1))
                inst = vector.tensor_scalar(wf[:, 3072:4096], wf[:, 2048:3072],
                                            wsm[:, ST + 1:ST + 2], None, OP.add)
                k += 1; cDV.inc(inst)
                vector.wait_ge(cAC.sem, pA_ac(t, 1))
                inst = vector.reduce_sum(wsm[:, ST + 2:ST + 3], wf[:, 4096:5120], AX.X)
                k += 1; cDV.inc(inst); vector.wait_ge(cDV.sem, cDV.n)
                inst = vector.tensor_scalar(wsm[:, ST + 3:ST + 4], wsm[:, ST + 2:ST + 3],
                                            1.0 / D, EPS_LN, OP.mult, OP.add)
                k += 1; cDV.inc(inst)
                vector.wait_ge(cAC.sem, pA_ac(t, 2))
                inst = vector.reciprocal(wsm[:, ST + 5:ST + 6], wsm[:, ST + 4:ST + 5])
                k += 1; cDV.inc(inst); vector.wait_ge(cDV.sem, cDV.n)
                inst = vector.tensor_scalar(wf[:, 3072:4096], wf[:, 3072:4096],
                                            wsm[:, ST + 5:ST + 6], None, OP.mult)
                k += 1; cDV.inc(inst)
                inst = vector.reduce_max(wsm[:, ST + 6:ST + 7], wf[:, 3072:4096], AX.X,
                                         apply_absolute_value=True)
                k += 1; cDV.inc(inst); vector.wait_ge(cDV.sem, cDV.n)
                inst = vector.tensor_scalar_max(wsm[:, ST + 6:ST + 7], wsm[:, ST + 6:ST + 7], EPS_Q)
                k += 1; cDV.inc(inst); vector.wait_ge(cDV.sem, cDV.n)
                inst = vector.reciprocal(wsm[:, ST + 7:ST + 8], wsm[:, ST + 6:ST + 7])
                k += 1; cDV.inc(inst); vector.wait_ge(cDV.sem, cDV.n)
                inst = vector.tensor_scalar_mul(wsm[:, ST + 7:ST + 8], wsm[:, ST + 7:ST + 8], 127.0)
                k += 1; cDV.inc(inst); vector.wait_ge(cDV.sem, cDV.n)
                inst = vector.tensor_tensor(wsm[:, S1C + t:S1C + t + 1],
                                            wsm[:, ST + 6:ST + 7],
                                            wsm[:, C_WQ1:C_WQ1 + 1], OP.mult)
                k += 1; cDV.inc(inst)
                for qch in range(2):
                    xsl = wf[:, 3072 + qch * 512:3072 + qch * 512 + 512]
                    qa = wf[:, 4096:4608]
                    qb = wf[:, 4608:5120]
                    qc = wf[:, 5120:5632]
                    inst = vector.tensor_scalar(qa, xsl, wsm[:, ST + 7:ST + 8], 256.0,
                                                OP.mult, OP.add)
                    k += 1; cDV.inc(inst)
                    inst = vector.tensor_copy(wri[:, :], qa)
                    k += 1; cDV.inc(inst)
                    inst = vector.tensor_copy(qb, wri[:, :])
                    k += 1; cDV.inc(inst)
                    inst = vector.tensor_sub(qa, qa, qb)
                    k += 1; cDV.inc(inst)
                    inst = vector.tensor_scalar(qc, qa, 0.5, None, OP.is_ge)
                    k += 1; cDV.inc(inst)
                    inst = vector.tensor_scalar(qa, qa, -0.5, None, OP.is_lt)
                    k += 1; cDV.inc(inst)
                    inst = vector.tensor_sub(qc, qc, qa)
                    k += 1; cDV.inc(inst)
                    inst = vector.scalar_tensor_tensor(
                        wkB[:, 4096 + qch * 512:4096 + qch * 512 + 512],
                        qb, -256.0, qc, OP.add, OP.add)
                    k += 1; cDV.inc(inst)
                assert k == 30
                vector.wait_ge(cPE.sem, pA_pe(t, 3))
                inst = vector.tensor_copy(
                    qarena.ap().rearrange("p (c n) -> p c n", n=2048)[:, 0:4, t * 128:t * 128 + 128],
                    psTa.ap().rearrange("p (c n) -> p c n", n=128))
                k += 1; cDV.inc(inst)
                inst = vector.tensor_copy(
                    qarena.ap().rearrange("p (c n) -> p c n", n=2048)[:, 4:8, t * 128:t * 128 + 128],
                    psTb.ap().rearrange("p (c n) -> p c n", n=128))
                k += 1; cDV.inc(inst)
                assert cDV.n == pA_dv(t, 32)

        @block.scalar
        def _(scalar):
            for t in range(NTT):
                scalar.wait_ge(cDV.sem, pA_dv(t, 5))
                inst = scalar.activation(wf[:, 4096:5120], wf[:, 3072:4096], AF.Square)
                cAC.inc(inst)
                scalar.wait_ge(cDV.sem, pA_dv(t, 7))
                inst = scalar.activation(wsm[:, ST + 4:ST + 5], wsm[:, ST + 3:ST + 4], AF.Sqrt)
                cAC.inc(inst)

        @block.gpsimd
        def _(g):
            for t in range(NTT):
                g.wait_ge(cDV.sem, pA_dv(t, 30))
                inst = g.dma_start(out=x1scr[t * 128:(t + 1) * 128, :], in_=wf[:, 3072:4096])
                qLG.inc(inst)

        qLD.n = ld0 + NTT * 3 * 16
        assert cPE.n == pe0 + NTT * 3
        assert cDV.n == dv0 + NTT * 32
        assert cAC.n == ac0 + NTT * 2
        X1DMA = [pA_x1dma(t) for t in range(NTT)]
        qLG.n = lg0 + NTT * 16
        DV_A = cDV.n

        ld0 = qLD.n
        pe0 = cPE.n
        dv0 = cDV.n
        ac0 = cAC.n

        lgB0 = qLG.n

        def pB_pe(t, k):
            return pe0 + t * 25 + k

        def pB_lg(t, k):
            return lgB0 + (t * 2 + k) * 16

        def pB_dv(t, k):
            return dv0 + t * 95 + k

        def pB_ac(t, k):
            return ac0 + t * 10 + k

        def pB_x1(t):
            return ld0 + (t + 1) * 16

        def pB_y(t):
            return (t + 1) * 16

        @block.sync
        def _(sync):
            for t in range(NTT):
                sync.wait_ge(cDV.sem, pB_dv(t, 78))
                sync.wait_ge(qLG.sem, X1DMA[t])
                inst = sync.dma_start(out=wf[:, 4096:5120], in_=x1scr[t * 128:(t + 1) * 128, :])
                qLD.inc(inst)

        @block.tensor
        def _(t_):
            t_.wait_ge(cDV.sem, DV_A)
            for t in range(NTT):
                for jh in range(2):
                    if jh == 1:
                        t_.wait_ge(cDV.sem, pB_dv(t, 4))
                    elif t >= 1:
                        t_.wait_ge(cDV.sem, pB_dv(t - 1, 8))
                    for dc in range(8):
                        lhs = qarena[:, dc * 2048 + t * 128:dc * 2048 + t * 128 + 128]
                        for sl in range(4):
                            inst = t_.matmul(
                                psBIG[:, sl * 512:sl * 512 + 512], lhs,
                                karena[:, dc * 4096 + jh * 2048 + sl * 512:
                                       dc * 4096 + jh * 2048 + sl * 512 + 512],
                                start=(dc == 0), stop=(dc == 7))
                            if dc == 7:
                                cPE.inc(inst)
                    assert cPE.n == pB_pe(t, jh * 4 + 4)
                t_.wait_ge(cDV.sem, pB_dv(t, 78))
                if t >= 1:
                    t_.wait_ge(cDV.sem, pB_dv(t - 1, 88))

                def _tr(grp):
                    if grp >= 2:
                        t_.wait_ge(cDV.sem, pB_dv(t, 80 + grp - 2))
                    elif t >= 1:
                        t_.wait_ge(cDV.sem, pB_dv(t - 1, 86 + grp))
                    dstp = psTa if grp % 2 == 0 else psTb
                    for i in range(4):
                        hc = grp * 4 + i
                        inst2 = t_.transpose(
                            dstp[:, i * 128:i * 128 + 128],
                            wkB[:, hc * 128:hc * 128 + 128], ident[:])
                    cPE.inc(inst2)

                def _mm(grp):
                    t_.wait_ge(cDV.sem, pB_dv(t, 80 + grp))
                    for i in range(4):
                        hc = grp * 4 + i
                        lhs = wkB[:, 4096 + (grp % 2) * 512 + i * 128:
                                  4096 + (grp % 2) * 512 + i * 128 + 128]
                        for half in range(2):
                            inst2 = t_.matmul(psC[:, half * 512:half * 512 + 512], lhs,
                                              varena[:, hc * 1024 + half * 512:
                                                     hc * 1024 + half * 512 + 512],
                                              start=(hc == 0), stop=False)
                    cPE.inc(inst2)

                _tr(0)
                _tr(1)
                for grp in range(2, 8):
                    _mm(grp - 2)
                    _tr(grp)
                _mm(6)
                _mm(7)
                t_.wait_ge(cDV.sem, pB_dv(t, 79))
                for half in range(2):
                    inst = t_.matmul(psC[:, half * 512:half * 512 + 512],
                                     s2invrow[0:1, :],
                                     b2row[0:1, half * 512:half * 512 + 512],
                                     start=False, stop=True)
                cPE.inc(inst)
                assert cPE.n == pB_pe(t, 25)

        @block.vector
        def _(vector):
            for t in range(NTT):
                k = 0
                for jh in range(2):
                    for sl in range(4):
                        gidx = jh * 4 + sl
                        vector.wait_ge(cPE.sem, pB_pe(t, gidx + 1))
                        if gidx >= 2:
                            vector.wait_ge(cAC.sem, pB_ac(t, gidx - 1))
                        elif t >= 1:
                            vector.wait_ge(cAC.sem, pB_ac(t - 1, 7 + gidx))
                        inst = vector.scalar_tensor_tensor(
                            wf[:, 4096 + (gidx % 2) * 512:4096 + (gidx % 2) * 512 + 512],
                            psBIG[:, sl * 512:sl * 512 + 512],
                            wsm[:, S1C + t:S1C + t + 1],
                            wkA[:, jh * 2048 + sl * 512:jh * 2048 + sl * 512 + 512],
                            OP.mult, OP.add)
                        k += 1; cDV.inc(inst)
                vector.wait_ge(cAC.sem, pB_ac(t, 8))
                inst = vector.reduce_max(wsm[:, ST:ST + 1], wf[:, 0:4096], AX.X,
                                         apply_absolute_value=True)
                k += 1; cDV.inc(inst); vector.wait_ge(cDV.sem, cDV.n)
                inst = vector.tensor_scalar_max(wsm[:, ST:ST + 1], wsm[:, ST:ST + 1], EPS_Q)
                k += 1; cDV.inc(inst); vector.wait_ge(cDV.sem, cDV.n)
                inst = vector.reciprocal(wsm[:, ST + 1:ST + 2], wsm[:, ST:ST + 1])
                k += 1; cDV.inc(inst); vector.wait_ge(cDV.sem, cDV.n)
                inst = vector.tensor_scalar_mul(wsm[:, ST + 1:ST + 2], wsm[:, ST + 1:ST + 2],
                                                127.0)
                k += 1; cDV.inc(inst); vector.wait_ge(cDV.sem, cDV.n)
                inst = vector.tensor_tensor(wsm[:, ST + 2:ST + 3], wsm[:, ST:ST + 1],
                                            wsm[:, C_WQ2:C_WQ2 + 1], OP.mult)
                k += 1; cDV.inc(inst); vector.wait_ge(cDV.sem, cDV.n)
                inst = vector.reciprocal(wsm[:, ST + 3:ST + 4], wsm[:, ST + 2:ST + 3])
                k += 1; cDV.inc(inst)
                assert k == 14
                for qch in range(8):
                    hsl = wf[:, qch * 512:qch * 512 + 512]
                    qa = wf[:, 4096:4608]
                    qb = wf[:, 4608:5120]
                    qc = wf[:, 5120:5632]
                    inst = vector.tensor_scalar(qa, hsl, wsm[:, ST + 1:ST + 2], 256.0,
                                                OP.mult, OP.add)
                    k += 1; cDV.inc(inst)
                    inst = vector.tensor_copy(wri[:, :], qa)
                    k += 1; cDV.inc(inst)
                    inst = vector.tensor_copy(qb, wri[:, :])
                    k += 1; cDV.inc(inst)
                    inst = vector.tensor_sub(qa, qa, qb)
                    k += 1; cDV.inc(inst)
                    inst = vector.tensor_scalar(qc, qa, 0.5, None, OP.is_ge)
                    k += 1; cDV.inc(inst)
                    inst = vector.tensor_scalar(qa, qa, -0.5, None, OP.is_lt)
                    k += 1; cDV.inc(inst)
                    inst = vector.tensor_sub(qc, qc, qa)
                    k += 1; cDV.inc(inst)
                    inst = vector.scalar_tensor_tensor(
                        wkB[:, qch * 512:qch * 512 + 512], qb, -256.0, qc, OP.add, OP.add)
                    k += 1; cDV.inc(inst)
                assert k == 78
                vector.wait_ge(qLG.sem, pB_lg(t, 2))
                inst = vector.tensor_copy(s2invrow[0:1, :], wsm[0:1, 64:192])
                k += 1; cDV.inc(inst)
                for grp in range(8):
                    tri = 9 + grp if grp < 2 else 8 + 2 * grp
                    vector.wait_ge(cPE.sem, pB_pe(t, tri))
                    if grp >= 2:
                        mmi = 11 + 2 * (grp - 2) if grp - 2 <= 5 else 17 + (grp - 2)
                        vector.wait_ge(cPE.sem, pB_pe(t, mmi))
                    elif t >= 1:
                        pmi = 17 + (grp + 6)
                        vector.wait_ge(cPE.sem, pB_pe(t - 1, pmi))
                    srcp = psTa if grp % 2 == 0 else psTb
                    inst = vector.tensor_copy(
                        wkB[:, 4096 + (grp % 2) * 512:4096 + (grp % 2) * 512 + 512],
                        srcp[:, 0:512])
                    k += 1; cDV.inc(inst)
                vector.wait_ge(qLD.sem, pB_x1(t))
                vector.wait_ge(cPE.sem, pB_pe(t, 25))
                if t >= 1:
                    vector.wait_ge(qOUT.sem, pB_y(t - 1))
                inst = vector.scalar_tensor_tensor(wf[:, 0:1024], psC[:, 0:1024],
                                                   wsm[:, ST + 2:ST + 3],
                                                   wf[:, 4096:5120], OP.mult, OP.add)
                k += 1; cDV.inc(inst)
                inst = vector.reduce_sum(wsm[:, ST + 4:ST + 5], wf[:, 0:1024], AX.X)
                k += 1; cDV.inc(inst); vector.wait_ge(cDV.sem, cDV.n)
                inst = vector.tensor_scalar_mul(wsm[:, ST + 5:ST + 6], wsm[:, ST + 4:ST + 5],
                                                -1.0 / D)
                k += 1; cDV.inc(inst); vector.wait_ge(cDV.sem, cDV.n)
                inst = vector.tensor_scalar(wf[:, 0:1024], wf[:, 0:1024],
                                            wsm[:, ST + 5:ST + 6], None, OP.add)
                k += 1; cDV.inc(inst)
                vector.wait_ge(cAC.sem, pB_ac(t, 9))
                inst = vector.reduce_sum(wsm[:, ST + 7:ST + 8], wf[:, 1024:2048], AX.X)
                k += 1; cDV.inc(inst); vector.wait_ge(cDV.sem, cDV.n)
                inst = vector.tensor_scalar(wsm[:, ST + 6:ST + 7], wsm[:, ST + 7:ST + 8],
                                            1.0 / D, EPS_LN, OP.mult, OP.add)
                k += 1; cDV.inc(inst)
                vector.wait_ge(cAC.sem, pB_ac(t, 10))
                inst = vector.reciprocal(wsm[:, ST + 5:ST + 6], wsm[:, ST + 7:ST + 8])
                k += 1; cDV.inc(inst); vector.wait_ge(cDV.sem, cDV.n)
                inst = vector.tensor_scalar(wf[:, 0:1024], wf[:, 0:1024],
                                            wsm[:, ST + 5:ST + 6], None, OP.mult)
                k += 1; cDV.inc(inst)
                assert k == 95
                assert cDV.n == pB_dv(t, 95)

        @block.scalar
        def _(scalar):
            for t in range(NTT):
                for jh in range(2):
                    for sl in range(4):
                        gidx = jh * 4 + sl
                        scalar.wait_ge(cDV.sem, pB_dv(t, gidx + 1))
                        if gidx < 2 and t >= 1:
                            scalar.wait_ge(qOUT.sem, pB_y(t - 1))
                        inst = scalar.activation(
                            wf[:, gidx * 512:gidx * 512 + 512],
                            wf[:, 4096 + (gidx % 2) * 512:4096 + (gidx % 2) * 512 + 512],
                            AF.Gelu)
                        cAC.inc(inst)
                scalar.wait_ge(cDV.sem, pB_dv(t, 91))
                inst = scalar.activation(wf[:, 1024:2048], wf[:, 0:1024], AF.Square)
                cAC.inc(inst)
                scalar.wait_ge(cDV.sem, pB_dv(t, 93))
                inst = scalar.activation(wsm[:, ST + 7:ST + 8], wsm[:, ST + 6:ST + 7], AF.Sqrt)
                cAC.inc(inst)
                assert cAC.n == pB_ac(t, 10)

        if debug:
            @block.vector
            def _(vector):
                vector.wait_ge(qOUT.sem, 0)
        @block.gpsimd
        def _(g):
            for t in range(NTT):
                g.wait_ge(cDV.sem, pB_dv(t, 14))
                if t >= 1:
                    g.wait_ge(cDV.sem, pB_dv(t - 1, 79))
                inst = g.dma_start(out=rscr.rearrange("(p a) -> p a", a=1),
                                   in_=wsm[:, ST + 3:ST + 4])
                qLG.inc(inst)
                g.wait_ge(qLG.sem, pB_lg(t, 1))
                inst = g.dma_start(out=wsm[0:1, 64:192],
                                   in_=rscr.rearrange("(a n) -> a n", a=1))
                qLG.inc(inst)
                g.wait_ge(cDV.sem, pB_dv(t, 95))
                inst = g.dma_start(out=y[t * 128:(t + 1) * 128, :], in_=wf[:, 0:1024])
                qOUT.inc(inst)
            if debug:
                qOUT.inc(g.dma_start(out=dbg_wsm[:], in_=wsm[:, :]))
                qOUT.inc(g.dma_start(out=dbg_x1[:], in_=x1scr[:]))
                qOUT.inc(g.dma_start(out=dbg_wf[:], in_=wf[:, :]))
                qOUT.inc(g.dma_start(out=dbg_wo[:], in_=wo_bf[:]))
                qOUT.inc(g.dma_start(out=dbg_q[:], in_=qarena[:, :]))
            g.wait_ge(qOUT.sem, qOUT.n)

        qLD.n = ld0 + NTT * 16
        assert cPE.n == pe0 + NTT * 25
        assert cDV.n == dv0 + NTT * 95
        assert cAC.n == ac0 + NTT * 10
        assert qLG.n == lgB0 + NTT * 2 * 16

    return nc


_CACHE = {}


def kernel(L0_attn_x, L0_attn_W_qkv, L0_attn_b_qkv, L0_attn_W_o, L0_attn_b_o,
           L0_ffn_W1, L0_ffn_b1, L0_ffn_W2, L0_ffn_b2):
    import ml_dtypes
    x = np.ascontiguousarray(np.asarray(L0_attn_x, dtype=np.float32))
    wqkv_ = np.ascontiguousarray(np.asarray(L0_attn_W_qkv, dtype=np.float32))
    bqkv_ = np.ascontiguousarray(np.asarray(L0_attn_b_qkv, dtype=np.float32))
    wo_ = np.ascontiguousarray(np.asarray(L0_attn_W_o, dtype=np.float32))
    bo_ = np.ascontiguousarray(np.asarray(L0_attn_b_o, dtype=np.float32))
    w1t_ = np.ascontiguousarray(np.asarray(L0_ffn_W1, dtype=np.float32).T)
    b1_ = np.ascontiguousarray(np.asarray(L0_ffn_b1, dtype=np.float32))
    w2t_ = np.ascontiguousarray(np.asarray(L0_ffn_W2, dtype=np.float32).T)
    b2_ = np.ascontiguousarray(np.asarray(L0_ffn_b2, dtype=np.float32))
    ident_ = np.eye(128, dtype=ml_dtypes.bfloat16)

    if "nc" not in _CACHE:
        _CACHE["nc"] = build_nc()
    nc = _CACHE["nc"]

    in_maps = []
    for c in range(8):
        b, half = c // 2, c % 2
        xT_ = np.ascontiguousarray(x[b].T)
        in_maps.append({
            "xT": xT_,
            "xTq": np.ascontiguousarray(xT_[:, half * QO:(half + 1) * QO]),
            "x_own": np.ascontiguousarray(x[b, half * QO:(half + 1) * QO]),
            "wqkv": wqkv_, "bqkv": bqkv_, "wo": wo_, "bo": bo_,
            "w1t": w1t_, "b1": b1_, "w2t": w2t_, "b2": b2_,
            "ident_in": ident_,
        })
    res = run_bass_kernel_spmd(nc, in_maps, core_ids=list(range(8)))
    out = np.empty((B, S, D), dtype=np.float32)
    for c in range(8):
        b, half = c // 2, c % 2
        out[b, half * QO:(half + 1) * QO] = res.results[c]["y"]
    return out
